# version 12
# baseline (speedup 1.0000x reference)
import math
import numpy as np
from contextlib import ExitStack

import concourse.bass as bass
import concourse.mybir as mybir
from concourse.bass_utils import run_bass_kernel_spmd
from concourse.aot_env import allow_ici_all_to_all

F32 = mybir.dt.float32
F32R = mybir.dt.float32r
ALU = mybir.AluOpType
ACT = mybir.ActivationFunctionType

D = 4096
S = 4096
B = 4
HD = 128
NH = 16
SH = S // 2
NC1 = 8
TC = 512
NDC = 32
NOT = 16
NU = 2
TCA = 1024
NU3 = 8
T3 = 256
EPS = 1e-5
C0 = 12.0
SCL = 1.0 / math.sqrt(HD)
GROUPS = [[0, 1], [2, 3], [4, 5], [6, 7]]

DBG = False



def _rope_tables():
    theta = 10000.0
    inv = 1.0 / theta ** (np.arange(0, HD, 2)[: HD // 2].astype(np.float32) / HD)
    t = np.arange(S)
    f = np.outer(t, inv).astype(np.float32)
    cos, sin = np.cos(f), np.sin(f)
    C2 = np.empty((HD, S), np.float32)
    S2 = np.empty((HD, S), np.float32)
    C2[0::2] = cos.T
    C2[1::2] = cos.T
    S2[0::2] = sin.T
    S2[1::2] = -sin.T
    return C2, S2


def _masks():
    m = np.zeros((HD, 4 * TC), np.float32)
    k = np.arange(128)[:, None]
    q = np.arange(TC)[None, :]
    for rel in range(4):
        m[:, rel * TC:(rel + 1) * TC] = (rel * 128 + k <= q).astype(np.float32)
    return m


def _host_inputs(x, wq, wk, wv, wo, route_w, A, Bw):
    C2, S2 = _rope_tables()
    maskT = _masks()
    Pmat = np.zeros((8, 128), np.float32)
    for e in range(8):
        Pmat[e, e * 16:(e + 1) * 16] = 1.0
    onesc = np.ones((128, 1), np.float32)
    onesr = np.ones((1, 128), np.float32)
    onesr8 = np.ones((1, 8), np.float32)
    ident = np.eye(128, dtype=np.float32)

    order = []
    for h in range(NH):
        order.append(wo[h * HD:(h + 1) * HD])
        order.append(wo[(NH + h) * HD:(NH + h + 1) * HD])
    wo_perm = np.concatenate(order, axis=0)
    wo_t = np.ascontiguousarray(
        wo_perm.reshape(32, 128, 32, 128).transpose(2, 1, 0, 3)
        .reshape(32, 128, 32 * 128))

    route_t = np.ascontiguousarray(
        route_w.T.reshape(32, 128, 8).transpose(1, 0, 2).reshape(128, 256))
    A_t = np.ascontiguousarray(
        A.reshape(128, D).T.reshape(32, 128, 128).transpose(1, 0, 2)
        .reshape(128, D))
    Bw_t = np.ascontiguousarray(Bw.transpose(0, 2, 1).reshape(128, D))

    maps = []
    for c in range(8):
        b, half = c // 2, c % 2
        cols = slice(half * 2048, half * 2048 + 2048)
        xT = np.ascontiguousarray(x[b].T)
        xc = np.ascontiguousarray(
            xT.reshape(32, 128, 8, TC).transpose(2, 1, 0, 3)
            .reshape(8, 128, 32 * TC))
        xTh = xT[:, half * SH:(half + 1) * SH]
        xrh = np.ascontiguousarray(
            xTh.reshape(32, 128, NU, TCA).transpose(2, 0, 1, 3))

        def wtile(w):
            return np.ascontiguousarray(
                w[:, cols].reshape(32, 128, 16, 128).transpose(2, 1, 0, 3)
                .reshape(16, 128, 32 * 128))
        maps.append({
            "xc": xc,
            "xrh": xrh,
            "wq": wtile(wq),
            "wk": wtile(wk),
            "wv": wtile(wv),
            "wo": wo_t,
            "route": route_t, "Am": A_t, "Bwm": Bw_t,
            "C2": C2, "S2": S2, "maskT": maskT, "Pmat": Pmat,
            "onesc": onesc, "onesr": onesr, "onesr8": onesr8, "ident": ident,
            "tok0": np.array([[half * SH]], np.int32),
        })
    return maps



def build_program():
    nc = bass.Bass("TRN2", target_bir_lowering=False,
                   detect_race_conditions=False)

    xc = nc.dram_tensor("xc", [NC1, 128, NDC * TC], F32R,
                        kind="ExternalInput").ap()
    xrh = nc.dram_tensor("xrh", [NU, NDC, 128, TCA], F32R,
                         kind="ExternalInput").ap()
    wq = nc.dram_tensor("wq", [NOT, 128, NDC * 128], F32R, kind="ExternalInput").ap()
    wk = nc.dram_tensor("wk", [NOT, 128, NDC * 128], F32R, kind="ExternalInput").ap()
    wv = nc.dram_tensor("wv", [NOT, 128, NDC * 128], F32R, kind="ExternalInput").ap()
    wo = nc.dram_tensor("wo", [32, 128, NDC * 128], F32R, kind="ExternalInput").ap()
    route = nc.dram_tensor("route", [128, NDC * 8], F32R, kind="ExternalInput").ap()
    Am = nc.dram_tensor("Am", [128, NDC * 128], F32R, kind="ExternalInput").ap()
    Bwm = nc.dram_tensor("Bwm", [128, NDC * 128], F32R, kind="ExternalInput").ap()
    C2h = nc.dram_tensor("C2", [HD, S], F32, kind="ExternalInput").ap()
    S2h = nc.dram_tensor("S2", [HD, S], F32, kind="ExternalInput").ap()
    maskTh = nc.dram_tensor("maskT", [HD, 4 * TC], F32, kind="ExternalInput").ap()
    Pmath = nc.dram_tensor("Pmat", [8, 128], F32R, kind="ExternalInput").ap()
    onesch = nc.dram_tensor("onesc", [128, 1], F32R, kind="ExternalInput").ap()
    onesrh = nc.dram_tensor("onesr", [1, 128], F32R, kind="ExternalInput").ap()
    onesr8h = nc.dram_tensor("onesr8", [1, 8], F32R, kind="ExternalInput").ap()
    identh = nc.dram_tensor("ident", [128, 128], F32R, kind="ExternalInput").ap()
    OUT = nc.dram_tensor("out", [D, SH], F32, kind="ExternalOutput").ap()

    kind_s = "ExternalOutput" if DBG else "Internal"
    QT = nc.dram_tensor("QT", [NH, 128, S], F32R, kind=kind_s).ap()
    KT = nc.dram_tensor("KT", [NH, 128, S], F32R, kind=kind_s).ap()
    VT = nc.dram_tensor("VT", [NH, 128, S], F32R, kind=kind_s).ap()
    OUT1 = nc.dram_tensor("OUT1", [D, SH], F32R, kind=kind_s).ap()
    a2a_in = nc.dram_tensor("a2a_in", [NH, 128, S], F32R, kind="Internal")
    a2a_out = nc.dram_tensor("a2a_out", [NH, 256, S], F32R, kind="Internal")
    tok0h = nc.dram_tensor("tok0", [1, 1], mybir.dt.int32,
                           kind="ExternalInput").ap()

    with ExitStack() as st:
        def sem(name):
            return st.enter_context(nc.semaphore(name))

        s_x = sem("s_x"); s_w = sem("s_w"); s_v2 = sem("s_v2")
        s_mm2 = sem("s_mm2"); s_vst = sem("s_vst"); s_sst = sem("s_sst")
        s_mmb = sem("s_mmb"); s_vxn = sem("s_vxn"); s_mmp = sem("s_mmp")
        s_ve = sem("s_ve"); s_scw = sem("s_scw")
        s_hl = sem("s_hl"); s_mk = sem("s_mk"); s_mmt = sem("s_mmt")
        s_vt = sem("s_vt"); s_mms = sem("s_mms"); s_act = sem("s_act")
        s_vm = sem("s_vm"); s_pv = sem("s_pv"); s_qc = sem("s_qc")
        s_vr = sem("s_vr"); s_mmo = sem("s_mmo"); s_vat = sem("s_vat")
        s_aw = sem("s_aw"); s_cc = sem("s_cc")
        s_al = sem("s_al"); s_xr = sem("s_xr"); s_w3 = sem("s_w3")
        s_mmP = sem("s_mmP"); s_vo1 = sem("s_vo1"); s_o1w = sem("s_o1w")
        s_ol = sem("s_ol"); s_ab = sem("s_ab"); s_v23 = sem("s_v23")
        s_m23 = sem("s_m23"); s_vst3 = sem("s_vst3"); s_sst3 = sem("s_sst3")
        s_mb3 = sem("s_mb3"); s_vx3 = sem("s_vx3"); s_mlg = sem("s_mlg")
        s_vlg = sem("s_vlg"); s_mrw = sem("s_mrw"); s_vrw = sem("s_vrw")
        s_mh = sem("s_mh"); s_vwh = sem("s_vwh"); s_mf3 = sem("s_mf3")
        s_vf3 = sem("s_vf3"); s_out = sem("s_out")

        block = st.enter_context(nc.Block())

        onesc_sb = st.enter_context(nc.sbuf_tensor("onesc_sb", [128, 1], F32R))
        onesr_sb = st.enter_context(nc.sbuf_tensor("onesr_sb", [1, 128], F32R))
        onesr8_sb = st.enter_context(nc.sbuf_tensor("onesr8_sb", [1, 8], F32R))
        ident_sb = st.enter_context(nc.sbuf_tensor("ident_sb", [128, 128], F32R))
        Pmat_sb = st.enter_context(nc.sbuf_tensor("Pmat_sb", [8, 128], F32R))
        route_sb = st.enter_context(nc.sbuf_tensor("route_sb", [128, NDC * 8], F32R))
        mu_sb = st.enter_context(nc.sbuf_tensor("mu_sb", [1, TC], F32R))
        m2_sb = st.enter_context(nc.sbuf_tensor("m2_sb", [1, TC], F32))
        mu2_sb = st.enter_context(nc.sbuf_tensor("mu2_sb", [1, TC], F32))
        var_sb = st.enter_context(nc.sbuf_tensor("var_sb", [1, TC], F32))
        rinv_sb = st.enter_context(nc.sbuf_tensor("rinv_sb", [1, TC], F32))
        rstd_sb = st.enter_context(nc.sbuf_tensor("rstd_sb", [1, TC], F32R))
        mub_sb = st.enter_context(nc.sbuf_tensor("mub_sb", [128, TC], F32))
        rstdb_sb = st.enter_context(nc.sbuf_tensor("rstdb_sb", [128, TC], F32))

        @block.gpsimd
        def _(g):
            g.dma_start(out=onesc_sb[:], in_=onesch).then_inc(s_scw, 16)
            g.dma_start(out=onesr_sb[:], in_=onesrh).then_inc(s_scw, 16)
            g.dma_start(out=onesr8_sb[:], in_=onesr8h).then_inc(s_scw, 16)
            g.dma_start(out=ident_sb[:], in_=identh).then_inc(s_scw, 16)
            g.dma_start(out=Pmat_sb[:], in_=Pmath).then_inc(s_scw, 16)
            g.dma_start(out=route_sb[:], in_=route).then_inc(s_scw, 16)
        NCONST = 6

        p1 = ExitStack()
        C2_sb = p1.enter_context(nc.sbuf_tensor("C2_sb", [HD, S], F32))
        S2_sb = p1.enter_context(nc.sbuf_tensor("S2_sb", [HD, S], F32))
        x_sb = p1.enter_context(nc.sbuf_tensor("x_sb", [128, NDC * TC], F32R))
        x2f_sb = p1.enter_context(nc.sbuf_tensor("x2f_sb", [128, 4 * TC], F32R))
        w_sb = p1.enter_context(nc.sbuf_tensor("w_sb", [128, 3 * NDC * 128], F32R))
        ebuf_sb = p1.enter_context(nc.sbuf_tensor("ebuf_sb", [128, 4 * TC], F32R))
        rt1_sb = p1.enter_context(nc.sbuf_tensor("rt1_sb", [128, TC], F32))
        rt2_sb = p1.enter_context(nc.sbuf_tensor("rt2_sb", [128, TC], F32))
        tmp_sb = p1.enter_context(nc.sbuf_tensor("tmp_sb", [128, TC], F32))
        ps_stat = p1.enter_context(nc.psum_tensor("ps_stat", [1, 1024], F32))
        ps_bc = p1.enter_context(nc.psum_tensor("ps_bc", [128, 1024], F32))
        ps_p = p1.enter_context(nc.psum_tensor("ps_p", [128, 2048], F32))

        NBLK = 3 * NOT
        W_ALL = [wq, wk, wv]
        SWAP = [i ^ 1 for i in range(32)]

        @block.sync
        def _(sync):
            for tc in range(NC1):
                if tc > 0:
                    sync.wait_ge(s_mmp, NBLK * tc)
                sync.dma_start(out=x_sb[:], in_=xc[tc]).then_inc(s_x, 16)

        @block.scalar
        def _(sc):
            sc.dma_start(out=C2_sb[:], in_=C2h).then_inc(s_w, 16)
            sc.dma_start(out=S2_sb[:], in_=S2h).then_inc(s_w, 16)
            for tc in range(NC1):
                for widx in range(3):
                    for ot in range(NOT):
                        g = tc * NBLK + widx * NOT + ot
                        if g >= 3:
                            sc.wait_ge(s_mmp, g - 2)
                        par = g % 3
                        sc.dma_start(
                            out=w_sb[:, par * NDC * 128:(par + 1) * NDC * 128],
                            in_=W_ALL[widx][ot],
                        ).then_inc(s_w, 16)
                        if widx == 0 and ot == 0:
                            sc.wait_ge(s_vst, tc + 1)
                            sc.activation(rstd_sb[:], rinv_sb[:], ACT.Sqrt
                                          ).then_inc(s_sst, 1)

        @block.vector
        def _(v):
            for tc in range(NC1):
                v.wait_ge(s_x, 16 * (tc + 1))
                for g4 in range(8):
                    gi = tc * 8 + g4
                    if gi > 0:
                        v.wait_ge(s_mm2, gi)
                    for j in range(4):
                        dc = g4 * 4 + j
                        ins = v.tensor_mul(
                            x2f_sb[:, (dc % 4) * TC:(dc % 4 + 1) * TC],
                            x_sb[:, dc * TC:(dc + 1) * TC].bitcast(F32),
                            x_sb[:, dc * TC:(dc + 1) * TC].bitcast(F32))
                    ins.then_inc(s_v2, 1)
                v.wait_ge(s_mm2, tc * 8 + 8)
                v.tensor_scalar(mu_sb[:], ps_stat[0:1, 0:TC], 1.0 / D, None,
                                ALU.mult)
                v.tensor_scalar(m2_sb[:], ps_stat[0:1, TC:2 * TC], 1.0 / D,
                                None, ALU.mult)
                v.tensor_mul(mu2_sb[:], mu_sb[:].bitcast(F32),
                             mu_sb[:].bitcast(F32))
                v.scalar_tensor_tensor(var_sb[:], m2_sb[:], EPS, mu2_sb[:],
                                       ALU.add, ALU.subtract)
                v.reciprocal(rinv_sb[:], var_sb[:]).then_inc(s_vst, 1)
                v.wait_ge(s_mmb, tc + 1)
                v.tensor_copy(mub_sb[:], ps_bc[:, 0:TC])
                v.tensor_copy(rstdb_sb[:], ps_bc[:, TC:2 * TC])
                for dc in range(NDC):
                    sl = x_sb[:, dc * TC:(dc + 1) * TC]
                    v.tensor_sub(tmp_sb[:], sl.bitcast(F32), mub_sb[:])
                    ins = v.tensor_mul(sl, tmp_sb[:], rstdb_sb[:])
                ins.then_inc(s_vxn, 1)
                for widx in range(3):
                    for ot in range(NOT):
                        g = tc * NBLK + widx * NOT + ot
                        v.wait_ge(s_mmp, g + 1)
                        if g >= 4:
                            v.wait_ge(s_scw, 16 * (g - 3) + 16 * NCONST)
                        par = g % 4
                        eb = ebuf_sb[:, (g % 4) * TC:(g % 4 + 1) * TC]
                        psrc = ps_p[:, par * TC:(par + 1) * TC]
                        if widx == 2:
                            ins = v.tensor_copy(eb, psrc)
                        else:
                            cs = C2_sb[:, tc * TC:(tc + 1) * TC]
                            ss = S2_sb[:, tc * TC:(tc + 1) * TC]
                            v.tensor_mul(rt1_sb[:], psrc, cs)
                            v.tensor_mul(rt2_sb[:], psrc, ss)
                            v.stream_shuffle(tmp_sb[:], rt2_sb[:], SWAP)
                            ins = v.tensor_add(eb, rt1_sb[:], tmp_sb[:])
                        ins.then_inc(s_ve, 1)

        @block.tensor
        def _(t):
            t.wait_ge(s_scw, 16 * NCONST)
            for tc in range(NC1):
                t.wait_ge(s_x, 16 * (tc + 1))
                for dc in range(NDC):
                    t.matmul(ps_stat[0:1, 0:TC], onesc_sb[:],
                             x_sb[:, dc * TC:(dc + 1) * TC],
                             start=(dc == 0), stop=(dc == NDC - 1))
                for g4 in range(8):
                    t.wait_ge(s_v2, tc * 8 + g4 + 1)
                    for j in range(4):
                        dc = g4 * 4 + j
                        ins = t.matmul(ps_stat[0:1, TC:2 * TC], onesc_sb[:],
                                       x2f_sb[:, (dc % 4) * TC:(dc % 4 + 1) * TC],
                                       start=(dc == 0), stop=(dc == NDC - 1))
                    ins.then_inc(s_mm2, 1)
                t.wait_ge(s_vst, tc + 1)
                t.wait_ge(s_sst, tc + 1)
                t.matmul(ps_bc[:, 0:TC], onesr_sb[:], mu_sb[:],
                         start=True, stop=True)
                t.matmul(ps_bc[:, TC:2 * TC], onesr_sb[:], rstd_sb[:],
                         start=True, stop=True).then_inc(s_mmb, 1)
                t.wait_ge(s_vxn, tc + 1)
                for widx in range(3):
                    for ot in range(NOT):
                        g = tc * NBLK + widx * NOT + ot
                        t.wait_ge(s_w, 16 * (g + 1) + 32)
                        if g >= 4:
                            t.wait_ge(s_ve, g - 3)
                        par = g % 4
                        for dc in range(NDC):
                            ins = t.matmul(
                                ps_p[:, par * TC:(par + 1) * TC],
                                w_sb[:, ((g % 3) * NDC + dc) * 128:
                                     ((g % 3) * NDC + dc + 1) * 128],
                                x_sb[:, dc * TC:(dc + 1) * TC],
                                start=(dc == 0), stop=(dc == NDC - 1))
                        ins.then_inc(s_mmp, 1)

        @block.gpsimd
        def _(g):
            SCR = [QT, KT, VT]
            for tc in range(NC1):
                for widx in range(3):
                    for ot in range(NOT):
                        gg = tc * NBLK + widx * NOT + ot
                        g.wait_ge(s_ve, gg + 1)
                        g.dma_start(
                            out=SCR[widx][ot, :, tc * TC:(tc + 1) * TC],
                            in_=ebuf_sb[:, (gg % 4) * TC:(gg % 4 + 1) * TC],
                        ).then_inc(s_scw, 16)

        P1_SCW = 16 * (NCONST + NC1 * NBLK)
        p1.close()

        p2 = ExitStack()
        qh_sb = p2.enter_context(nc.sbuf_tensor("qh_sb", [128, 2 * S], F32R))
        kh_sb = p2.enter_context(nc.sbuf_tensor("kh_sb", [128, 2 * S], F32R))
        vh_sb = p2.enter_context(nc.sbuf_tensor("vh_sb", [128, 2 * S], F32R))
        vtm_sb = p2.enter_context(nc.sbuf_tensor("vtm_sb", [128, S], F32R))
        maskT_sb = p2.enter_context(nc.sbuf_tensor("maskT_sb", [HD, 4 * TC], F32))
        prb_sb = p2.enter_context(nc.sbuf_tensor("prb_sb", [128, 4 * TC], F32R))
        attb_sb = p2.enter_context(nc.sbuf_tensor("attb_sb", [128, 2 * TC], F32R))
        linv_sb = p2.enter_context(nc.sbuf_tensor("linv_sb", [1, TC], F32R))
        lb_sb = p2.enter_context(nc.sbuf_tensor("lb_sb", [128, TC], F32))
        ps_s = p2.enter_context(nc.psum_tensor("ps_s", [128, 4 * TC], F32))
        ps_att = p2.enter_context(nc.psum_tensor("ps_att", [128, 2 * TC], F32))
        ps_l = p2.enter_context(nc.psum_tensor("ps_l", [1, TC], F32))
        ps_b = p2.enter_context(nc.psum_tensor("ps_b", [128, TC], F32))
        ps_t = ps_s.bitcast(F32R) if False else None
        cbias_sb = p2.enter_context(nc.sbuf_tensor("cbias_sb", [128, 1], F32))
        cscl_sb = p2.enter_context(nc.sbuf_tensor("cscl_sb", [128, 1], F32))
        csm1_sb = p2.enter_context(nc.sbuf_tensor("csm1_sb", [128, 1], F32))
        lnt_sb = p2.enter_context(nc.sbuf_tensor("lnt_sb", [1, TC], F32))

        KTS = [4 * qc + 4 for qc in range(8)]
        KT_HEAD = sum(KTS)
        PRS = [k // 2 for k in KTS]
        PR_HEAD = sum(PRS)

        def kt_base(h, qc):
            return h * KT_HEAD + sum(KTS[:qc])

        def pr_base(h, qc):
            return h * PR_HEAD + sum(PRS[:qc])

        @block.sync
        def _(sync):
            sync.wait_ge(s_scw, P1_SCW)
            for h in range(NH):
                par = h % 2
                if h >= 2:
                    sync.wait_ge(s_qc, 8 * (h - 1))
                sync.dma_start(out=qh_sb[:, par * S:(par + 1) * S],
                               in_=QT[h]).then_inc(s_hl, 16)
                sync.dma_start(out=kh_sb[:, par * S:(par + 1) * S],
                               in_=KT[h]).then_inc(s_hl, 16)
                sync.dma_start(out=vh_sb[:, par * S:(par + 1) * S],
                               in_=VT[h]).then_inc(s_hl, 16)

        @block.scalar
        def _(sc):
            sc.wait_ge(s_scw, P1_SCW)
            sc.dma_start(out=maskT_sb[:], in_=maskTh).then_inc(s_mk, 16)
            sc.wait_ge(s_mk, 17)
            for h in range(NH):
                for qc in range(8):
                    for j in range(PRS[qc]):
                        gp = pr_base(h, qc) + j
                        sc.wait_ge(s_mms, gp + 1)
                        if gp >= 2:
                            sc.wait_ge(s_pv, 2 * (gp - 1))
                        sl2 = slice((gp % 2) * 2 * TC, (gp % 2 + 1) * 2 * TC)
                        sc.activation(prb_sb[:, sl2], ps_s[:, sl2],
                                      ACT.Exp, bias=cbias_sb[:],
                                      scale=cscl_sb[:]).then_inc(s_act, 1)
                    nq = h * 8 + qc
                    sc.wait_ge(s_qc, nq + 1)
                    sc.activation(lnt_sb[:], ps_l[:], ACT.Ln)
                    sc.activation(linv_sb[:], lnt_sb[:], ACT.Exp,
                                  scale=csm1_sb[0:1, :]).then_inc(s_vr, 1)

        @block.vector
        def _(v):
            v.wait_ge(s_scw, P1_SCW)
            v.wait_ge(s_mk, 16)
            v.memset(cbias_sb[:], -C0)
            v.memset(csm1_sb[:], -1.0)
            v.memset(cscl_sb[:], SCL).then_inc(s_mk, 1)
            for h in range(NH):
                for j in range(NDC):
                    v.wait_ge(s_mmt, h * NDC + j + 1)
                    v.tensor_copy(vtm_sb[:, j * 128:(j + 1) * 128],
                                  ps_s[:, (j % 2) * 512:(j % 2) * 512 + 128])
                    v.engine_nop().then_inc(s_vt, 1)
                for qc in range(8):
                    nq = h * 8 + qc
                    for rel in range(4):
                        kt = 4 * qc + rel
                        n = kt_base(h, qc) + kt
                        v.wait_ge(s_act, pr_base(h, qc) + kt // 2 + 1)
                        gp = pr_base(h, qc) + kt // 2
                        off = ((gp % 2) * 2 + kt % 2) * TC
                        sl = prb_sb[:, off:off + TC]
                        v.tensor_mul(sl, sl.bitcast(F32),
                                     maskT_sb[:, rel * TC:(rel + 1) * TC]
                                     ).then_inc(s_vm, 1)
                    v.wait_ge(s_mmo, nq + 1)
                    v.tensor_copy(lb_sb[:], ps_b[:])
                    ab = attb_sb[:, (qc % 2) * TC:(qc % 2 + 1) * TC]
                    if nq >= 2:
                        v.wait_ge(s_aw, 16 * (nq - 1))
                    v.wait_ge(s_pv, kt_base(h, qc) + KTS[qc])
                    v.tensor_mul(ab, ps_att[:, (qc % 2) * TC:(qc % 2 + 1) * TC],
                                 lb_sb[:]).then_inc(s_vat, 1)

        def t_wait_probs(t, h, qc, kt, n):
            if kt >= 4 * qc:
                ndv = (h * 8 + qc) * 4 + (kt - 4 * qc) + 1
                t.wait_ge(s_vm, ndv)
            else:
                t.wait_ge(s_act, pr_base(h, qc) + kt // 2 + 1)

        def emit_lsum_pv(t, h, qc, kt, n, nq):
            gp = pr_base(h, qc) + kt // 2
            off = ((gp % 2) * 2 + kt % 2) * TC
            prb = prb_sb[:, off:off + TC]
            nkts = KTS[qc]
            if kt == 0:
                if nq >= 1:
                    t.wait_ge(s_vr, nq)
                if nq >= 2:
                    t.wait_ge(s_vat, nq - 1)
            ins = t.matmul(ps_l[:], onesc_sb[:], prb,
                           start=(kt == 0), stop=(kt == nkts - 1))
            if kt == nkts - 1:
                ins.then_inc(s_qc, 1)
            t.matmul(ps_att[:, (qc % 2) * TC:(qc % 2 + 1) * TC],
                     vtm_sb[:, kt * 128:(kt + 1) * 128], prb,
                     start=(kt == 0), stop=(kt == nkts - 1)).then_inc(s_pv, 1)

        @block.tensor
        def _(t):
            for h in range(NH):
                par = h % 2
                t.wait_ge(s_hl, 48 * (h + 1))
                if h >= 1:
                    t.wait_ge(s_act, h * PR_HEAD)
                for j in range(NDC):
                    if h * NDC + j >= 2:
                        t.wait_ge(s_vt, h * NDC + j - 1)
                    t.transpose(ps_s.bitcast(F32R)[:, (j % 2) * 512:
                                                   (j % 2) * 512 + 128],
                                vh_sb[:, par * S + j * 128:
                                      par * S + (j + 1) * 128],
                                ident_sb[:]).then_inc(s_mmt, 1)
                t.wait_ge(s_vt, (h + 1) * NDC)
                for qc in range(8):
                    nq = h * 8 + qc
                    nprs = PRS[qc]
                    for j in range(nprs):
                        gp = pr_base(h, qc) + j
                        if gp >= 2:
                            t.wait_ge(s_act, gp - 1)
                        for half2 in range(2):
                            kt = 2 * j + half2
                            ins = t.matmul(
                                ps_s[:, ((gp % 2) * 2 + half2) * TC:
                                     ((gp % 2) * 2 + half2 + 1) * TC],
                                kh_sb[:, par * S + kt * 128:
                                      par * S + (kt + 1) * 128],
                                qh_sb[:, par * S + qc * TC:
                                      par * S + (qc + 1) * TC],
                                start=True, stop=True)
                        ins.then_inc(s_mms, 1)
                        if j >= 1:
                            for half2 in range(2):
                                kt = 2 * (j - 1) + half2
                                n = kt_base(h, qc) + kt
                                t_wait_probs(t, h, qc, kt, n)
                                emit_lsum_pv(t, h, qc, kt, n, nq)
                        if j == 1 and qc >= 1:
                            t.wait_ge(s_vr, nq)
                            t.wait_ge(s_vat, nq - 1 if nq >= 2 else 0)
                            t.matmul(ps_b[:], onesr_sb[:], linv_sb[:],
                                     start=True, stop=True).then_inc(s_mmo, 1)
                    for half2 in range(2):
                        kt = 2 * (nprs - 1) + half2
                        n = kt_base(h, qc) + kt
                        t_wait_probs(t, h, qc, kt, n)
                        emit_lsum_pv(t, h, qc, kt, n, nq)
                t.wait_ge(s_vr, 8 * (h + 1))
                t.wait_ge(s_vat, 8 * (h + 1) - 1)
                t.matmul(ps_b[:], onesr_sb[:], linv_sb[:],
                         start=True, stop=True).then_inc(s_mmo, 1)

        @block.gpsimd
        def _(g):
            for h in range(NH):
                for qc in range(8):
                    nq = h * 8 + qc
                    g.wait_ge(s_vat, nq + 1)
                    g.dma_start(
                        out=a2a_in.ap()[h, :, qc * TC:(qc + 1) * TC],
                        in_=attb_sb[:, (qc % 2) * TC:(qc % 2 + 1) * TC],
                    ).then_inc(s_aw, 16)
                g.wait_ge(s_aw, 16 * 8 * (h + 1))
                g.collective_compute(
                    "AllGather", ALU.bypass,
                    replica_groups=GROUPS,
                    ins=[a2a_in.ap()[h].opt()],
                    outs=[a2a_out.ap()[h].opt()],
                ).then_inc(s_cc, 1)

        p2.close()

        p3 = ExitStack()
        att_sb = p3.enter_context(nc.sbuf_tensor("att_sb", [128, NDC * TCA], F32R))
        wod_sb = p3.enter_context(nc.sbuf_tensor("wod_sb", [128, 2 * NDC * 128], F32R))
        xr_sb = p3.enter_context(nc.sbuf_tensor("xr_sb", [128, 2 * TCA], F32R))
        eb3_sb = p3.enter_context(nc.sbuf_tensor("eb3_sb", [128, 2 * TCA], F32R))
        ps_P = p3.enter_context(nc.psum_tensor("ps_P", [128, 2048], F32))

        tok0_sb = p3.enter_context(nc.sbuf_tensor("tok0_sb", [1, 1],
                                                   mybir.dt.int32))

        @block.sync
        def _(sync):
            sync.wait_ge(s_cc, NH)
            for u in range(NU):
                for dcol in range(NDC):
                    g3 = u * NDC + dcol
                    if g3 >= 2:
                        sync.wait_ge(s_vo1, g3 - 1)
                    sync.dma_start(
                        out=xr_sb[:, (dcol % 2) * TCA:(dcol % 2 + 1) * TCA],
                        in_=xrh[u, dcol],
                    ).then_inc(s_xr, 16)

        @block.scalar
        def _(sc):
            sc.wait_ge(s_cc, NH)
            for u in range(NU):
                for dcol in range(NDC):
                    g3 = u * NDC + dcol
                    if g3 >= 2:
                        sc.wait_ge(s_mmP, g3 - 1)
                    par = g3 % 2
                    sc.dma_start(
                        out=wod_sb[:, par * NDC * 128:(par + 1) * NDC * 128],
                        in_=wo[dcol],
                    ).then_inc(s_w3, 16)

        @block.tensor
        def _(t):
            for u in range(NU):
                t.wait_ge(s_al, 16 * (u + 1) + 16)
                for dcol in range(NDC):
                    g3 = u * NDC + dcol
                    t.wait_ge(s_w3, 16 * (g3 + 1))
                    if g3 >= 2:
                        t.wait_ge(s_vo1, g3 - 1)
                    par = g3 % 2
                    for hf in range(2):
                        for rc in range(NDC):
                            ins = t.matmul(
                                ps_P[:, (par * 2 + hf) * TC:
                                     (par * 2 + hf + 1) * TC],
                                wod_sb[:, (par * NDC + rc) * 128:
                                       (par * NDC + rc + 1) * 128],
                                att_sb[:, rc * TCA + hf * TC:
                                       rc * TCA + (hf + 1) * TC],
                                start=(rc == 0), stop=(rc == NDC - 1))
                    ins.then_inc(s_mmP, 1)

        @block.vector
        def _(v):
            v.wait_ge(s_cc, NH)
            for u in range(NU):
                for dcol in range(NDC):
                    g3 = u * NDC + dcol
                    v.wait_ge(s_mmP, g3 + 1)
                    v.wait_ge(s_xr, 16 * (g3 + 1))
                    if g3 >= 2:
                        v.wait_ge(s_o1w, 32 * (g3 - 1))
                    par = g3 % 2
                    v.tensor_add(eb3_sb[:, (g3 % 2) * TCA:(g3 % 2 + 1) * TCA],
                                 ps_P[:, par * 2 * TC:(par + 1) * 2 * TC],
                                 xr_sb[:, (dcol % 2) * TCA:(dcol % 2 + 1) * TCA]
                                 .bitcast(F32)).then_inc(s_vo1, 1)

        @block.gpsimd
        def _(g):
            g.wait_ge(s_cc, NH)
            g.dma_start(out=tok0_sb[:], in_=tok0h).then_inc(s_al, 16)
            g.wait_ge(s_al, 16)
            with g.register("gr3") as gr3:
                g.reg_load(gr3, tok0_sb[0:1, 0:1])
                for u in range(NU):
                    if u > 0:
                        g.wait_ge(s_mmP, NDC * u)
                    def att_src(w0):
                        return a2a_out.ap()[:, :, w0 + u * TCA:
                                            w0 + (u + 1) * TCA].rearrange(
                            "h (b p) t -> h b p t", p=128
                        ).transpose([2, 0, 1, 3])
                    with g.If_eq(gr3, 0):
                        g.dma_start(out=att_sb[:], in_=att_src(0)
                                    ).then_inc(s_al, 16)
                    with g.Else():
                        g.dma_start(out=att_sb[:], in_=att_src(SH)
                                    ).then_inc(s_al, 16)
                    for dcol in range(NDC):
                        g3 = u * NDC + dcol
                        g.wait_ge(s_vo1, g3 + 1)
                        g.dma_start(
                            out=OUT1[dcol * 128:(dcol + 1) * 128,
                                     u * TCA:(u + 1) * TCA],
                            in_=eb3_sb[:, (g3 % 2) * TCA:(g3 % 2 + 1) * TCA],
                        ).then_inc(s_o1w, 32)

        P3A_O1W = 32 * NU * NDC
        p3.close()

        p4 = ExitStack()
        o1_sb = p4.enter_context(nc.sbuf_tensor("o1_sb", [128, 2 * NDC * T3], F32R))
        xn2_sb = p4.enter_context(nc.sbuf_tensor("xn2_sb", [128, NDC * T3], F32R))
        ab_sb = p4.enter_context(nc.sbuf_tensor("ab_sb", [128, 2 * NDC * 128], F32R))
        x23_sb = p4.enter_context(nc.sbuf_tensor("x23_sb", [128, 4 * T3], F32R))
        lgp_sb = p4.enter_context(nc.sbuf_tensor("lgp_sb", [32, T3], F32))
        lgs_sb = p4.enter_context(nc.sbuf_tensor("lgs_sb", [32, T3], F32))
        mx_sb = p4.enter_context(nc.sbuf_tensor("mx_sb", [32, T3], F32))
        m1r_sb = p4.enter_context(nc.sbuf_tensor("m1r_sb", [1, T3], F32R))
        mb8_sb = p4.enter_context(nc.sbuf_tensor("mb8_sb", [8, T3], F32))
        rwm_sb = p4.enter_context(nc.sbuf_tensor("rwm_sb", [8, T3], F32))
        rw_sb = p4.enter_context(nc.sbuf_tensor("rw_sb", [8, T3], F32R))
        rwb_sb = p4.enter_context(nc.sbuf_tensor("rwb_sb", [128, T3], F32))
        wh_sb = p4.enter_context(nc.sbuf_tensor("wh_sb", [128, T3], F32R))
        tmp3_sb = p4.enter_context(nc.sbuf_tensor("tmp3_sb", [128, T3], F32))
        fin_sb = p4.enter_context(nc.sbuf_tensor("fin_sb", [128, NDC * T3], F32))
        ps_f = p4.enter_context(nc.psum_tensor("ps_f", [128, 1024], F32))
        ps_st3 = p4.enter_context(nc.psum_tensor("ps_st3", [1, 1024], F32))
        ps_bc3 = p4.enter_context(nc.psum_tensor("ps_bc3", [128, 1024], F32))
        ps_ms = p4.enter_context(nc.psum_tensor("ps_ms", [128, 512], F32))

        NG3 = 8

        @block.gpsimd
        def _(g):
            g.wait_ge(s_o1w, P3A_O1W)
            g.dma_start(out=ab_sb[:, 0:NDC * 128],
                        in_=Am).then_inc(s_ab, 16)
            g.dma_start(out=ab_sb[:, NDC * 128:2 * NDC * 128],
                        in_=Bwm).then_inc(s_ab, 16)
            for u in range(NU3):
                g.wait_ge(s_vf3, NDC * (u + 1))
                g.dma_start(
                    out=OUT[:, u * T3:(u + 1) * T3].rearrange(
                        "(c p) t -> c p t", p=128).transpose([1, 0, 2]),
                    in_=fin_sb[:],
                ).then_inc(s_out, 16)
            g.wait_ge(s_out, 16 * NU3)

        @block.sync
        def _(sync):
            sync.wait_ge(s_o1w, P3A_O1W)
            for u in range(NU3):
                if u > 1:
                    sync.wait_ge(s_vf3, NDC * (u - 1))
                sync.dma_start(
                    out=o1_sb[:, (u % 2) * NDC * T3:(u % 2 + 1) * NDC * T3],
                    in_=OUT1[:, u * T3:(u + 1) * T3].rearrange(
                        "(c p) t -> c p t", p=128).transpose([1, 0, 2]),
                ).then_inc(s_ol, 16)

        @block.scalar
        def _(sc):
            for u in range(NU3):
                sc.wait_ge(s_vst3, u + 1)
                sc.activation(rstd_sb[0:1, 0:T3], rinv_sb[0:1, 0:T3],
                              ACT.Sqrt).then_inc(s_sst3, 1)

        @block.vector
        def _(v):
            for u in range(NU3):
                v.wait_ge(s_ol, 16 * (u + 1))
                for g4 in range(NG3):
                    gi3 = u * NG3 + g4
                    if gi3 > 0:
                        v.wait_ge(s_m23, gi3)
                    for j in range(4):
                        dc = g4 * 4 + j
                        o1t = o1_sb[:, ((u % 2) * NDC + dc) * T3:
                                    ((u % 2) * NDC + dc + 1) * T3]
                        ins = v.tensor_mul(
                            x23_sb[:, (dc % 4) * T3:(dc % 4 + 1) * T3],
                            o1t.bitcast(F32), o1t.bitcast(F32))
                    ins.then_inc(s_v23, 1)
                v.wait_ge(s_m23, u * NG3 + NG3)
                v.tensor_scalar(mu_sb[0:1, 0:T3], ps_st3[0:1, 0:T3],
                                1.0 / D, None, ALU.mult)
                v.tensor_scalar(m2_sb[0:1, 0:T3], ps_st3[0:1, TC:TC + T3],
                                1.0 / D, None, ALU.mult)
                v.tensor_mul(mu2_sb[0:1, 0:T3], mu_sb[0:1, 0:T3].bitcast(F32),
                             mu_sb[0:1, 0:T3].bitcast(F32))
                v.scalar_tensor_tensor(var_sb[0:1, 0:T3], m2_sb[0:1, 0:T3],
                                       EPS, mu2_sb[0:1, 0:T3],
                                       ALU.add, ALU.subtract)
                v.reciprocal(rinv_sb[0:1, 0:T3], var_sb[0:1, 0:T3]
                             ).then_inc(s_vst3, 1)
                v.wait_ge(s_mb3, u + 1)
                v.tensor_copy(mub_sb[:, 0:T3], ps_bc3[:, 0:T3])
                v.tensor_copy(rstdb_sb[:, 0:T3], ps_bc3[:, TC:TC + T3])
                for dc in range(NDC):
                    v.tensor_sub(tmp3_sb[:],
                                 o1_sb[:, ((u % 2) * NDC + dc) * T3:
                                       ((u % 2) * NDC + dc + 1) * T3]
                                 .bitcast(F32),
                                 mub_sb[:, 0:T3])
                    ins = v.tensor_mul(xn2_sb[:, dc * T3:(dc + 1) * T3],
                                       tmp3_sb[:], rstdb_sb[:, 0:T3])
                ins.then_inc(s_vx3, 1)
                v.wait_ge(s_mlg, u + 1)
                v.memset(lgp_sb[:], -3.0e38)
                v.tensor_copy(lgp_sb[0:8, :], ps_ms[0:8, 0:T3])
                v.stream_shuffle(lgs_sb[:], lgp_sb[:],
                                 [min(i + 4, 31) for i in range(32)])
                v.tensor_max(mx_sb[:], lgp_sb[:], lgs_sb[:])
                v.stream_shuffle(lgs_sb[:], mx_sb[:],
                                 [min(i + 2, 31) for i in range(32)])
                v.tensor_max(mx_sb[:], mx_sb[:], lgs_sb[:])
                v.stream_shuffle(lgs_sb[:], mx_sb[:],
                                 [min(i + 1, 31) for i in range(32)])
                v.tensor_max(mx_sb[:], mx_sb[:], lgs_sb[:])
                v.stream_shuffle(lgs_sb[:], mx_sb[:], [0] * 32)
                v.tensor_tensor(rwm_sb[:], lgp_sb[0:8, :], lgs_sb[0:8, :],
                                ALU.is_equal)
                v.tensor_mul(rw_sb[:], rwm_sb[:], lgs_sb[0:8, :]
                             ).then_inc(s_vlg, 1)
                v.wait_ge(s_mrw, u + 1)
                v.tensor_copy(rwb_sb[:], ps_ms[:, 0:T3]).then_inc(s_vrw, 1)
                v.wait_ge(s_mh, u + 1)
                v.tensor_mul(wh_sb[:], ps_bc3[:, 0:T3], rwb_sb[:]
                             ).then_inc(s_vwh, 1)
                if u > 0:
                    v.wait_ge(s_out, 16 * u)
                for ot in range(NDC):
                    nf = u * NDC + ot
                    v.wait_ge(s_mf3, nf + 1)
                    par = ot % 2
                    v.tensor_add(fin_sb[:, ot * T3:(ot + 1) * T3],
                                 ps_f[:, par * TC:par * TC + T3],
                                 o1_sb[:, ((u % 2) * NDC + ot) * T3:
                                       ((u % 2) * NDC + ot + 1) * T3]
                                 .bitcast(F32)).then_inc(s_vf3, 1)

        @block.tensor
        def _(t):
            t.wait_ge(s_ab, 32)
            for u in range(NU3):
                t.wait_ge(s_ol, 16 * (u + 1))
                for dc in range(NDC):
                    t.matmul(ps_st3[0:1, 0:T3], onesc_sb[:],
                             o1_sb[:, ((u % 2) * NDC + dc) * T3:
                                   ((u % 2) * NDC + dc + 1) * T3],
                             start=(dc == 0), stop=(dc == NDC - 1))
                for g4 in range(NG3):
                    t.wait_ge(s_v23, u * NG3 + g4 + 1)
                    for j in range(4):
                        dc = g4 * 4 + j
                        ins = t.matmul(ps_st3[0:1, TC:TC + T3], onesc_sb[:],
                                       x23_sb[:, (dc % 4) * T3:(dc % 4 + 1) * T3],
                                       start=(dc == 0), stop=(dc == NDC - 1))
                    ins.then_inc(s_m23, 1)
                t.wait_ge(s_vst3, u + 1)
                t.wait_ge(s_sst3, u + 1)
                t.matmul(ps_bc3[:, 0:T3], onesr_sb[:], mu_sb[0:1, 0:T3],
                         start=True, stop=True)
                t.matmul(ps_bc3[:, TC:TC + T3], onesr_sb[:],
                         rstd_sb[0:1, 0:T3],
                         start=True, stop=True).then_inc(s_mb3, 1)
                t.wait_ge(s_vx3, u + 1)
                for dc in range(NDC):
                    ins = t.matmul(ps_ms[0:8, 0:T3],
                                   route_sb[:, dc * 8:(dc + 1) * 8],
                                   xn2_sb[:, dc * T3:(dc + 1) * T3],
                                   start=(dc == 0), stop=(dc == NDC - 1))
                ins.then_inc(s_mlg, 1)
                t.wait_ge(s_vlg, u + 1)
                t.matmul(ps_ms[:, 0:T3], Pmat_sb[:], rw_sb[:],
                         start=True, stop=True).then_inc(s_mrw, 1)
                t.wait_ge(s_vrw, u + 1)
                for dc in range(NDC):
                    ins = t.matmul(ps_bc3[:, 0:T3],
                                   ab_sb[:, dc * 128:(dc + 1) * 128],
                                   xn2_sb[:, dc * T3:(dc + 1) * T3],
                                   start=(dc == 0), stop=(dc == NDC - 1))
                ins.then_inc(s_mh, 1)
                t.wait_ge(s_vwh, u + 1)
                for ot in range(NDC):
                    nf = u * NDC + ot
                    if nf >= 2:
                        t.wait_ge(s_vf3, nf - 1)
                    par = ot % 2
                    t.matmul(ps_f[:, par * TC:par * TC + T3],
                             ab_sb[:, (NDC + ot) * 128:(NDC + ot + 1) * 128],
                             wh_sb[:], start=True, stop=True
                             ).then_inc(s_mf3, 1)

        p4.close()

    return nc



def kernel(L0_attn_x, L0_attn_wq, L0_attn_wk, L0_attn_wv, L0_attn_wo,
           L0_ffn_route_w, L0_ffn_A, L0_ffn_Bw):
    x = np.ascontiguousarray(np.asarray(L0_attn_x, np.float32))
    maps = _host_inputs(
        x,
        np.ascontiguousarray(np.asarray(L0_attn_wq, np.float32)),
        np.ascontiguousarray(np.asarray(L0_attn_wk, np.float32)),
        np.ascontiguousarray(np.asarray(L0_attn_wv, np.float32)),
        np.ascontiguousarray(np.asarray(L0_attn_wo, np.float32)),
        np.ascontiguousarray(np.asarray(L0_ffn_route_w, np.float32)),
        np.ascontiguousarray(np.asarray(L0_ffn_A, np.float32)),
        np.ascontiguousarray(np.asarray(L0_ffn_Bw, np.float32)))
    nc = build_program()
    res = run_bass_kernel_spmd(nc, maps, core_ids=list(range(8)))
    out = np.empty((B, S, D), np.float32)
    for b in range(B):
        lo = res.results[2 * b]["out"]
        hi = res.results[2 * b + 1]["out"]
        out[b] = np.concatenate([lo, hi], axis=1).T
    return out


# version 14 (final)
# speedup vs baseline: 1.0635x; 1.0635x over previous
import math
import numpy as np
from contextlib import ExitStack

import concourse.bass as bass
import concourse.mybir as mybir
from concourse.bass_utils import run_bass_kernel_spmd
from concourse.aot_env import allow_ici_all_to_all

F32 = mybir.dt.float32
F32R = mybir.dt.float32r
ALU = mybir.AluOpType
ACT = mybir.ActivationFunctionType

D = 4096
S = 4096
B = 4
HD = 128
NH = 16
SH = S // 2
NC1 = 8
TC = 512
NDC = 32
NOT = 16
NU = 2
TCA = 1024
NU3 = 8
T3 = 256
EPS = 1e-5
C0 = 12.0
SCL = 1.0 / math.sqrt(HD)
GROUPS = [[0, 1], [2, 3], [4, 5], [6, 7]]

DBG = False



def _rope_tables():
    theta = 10000.0
    inv = 1.0 / theta ** (np.arange(0, HD, 2)[: HD // 2].astype(np.float32) / HD)
    t = np.arange(S)
    f = np.outer(t, inv).astype(np.float32)
    cos, sin = np.cos(f), np.sin(f)
    C2 = np.empty((HD, S), np.float32)
    S2 = np.empty((HD, S), np.float32)
    C2[0::2] = cos.T
    C2[1::2] = cos.T
    S2[0::2] = sin.T
    S2[1::2] = -sin.T
    return C2, S2


def _masks():
    m = np.zeros((HD, 4 * TC), np.float32)
    k = np.arange(128)[:, None]
    q = np.arange(TC)[None, :]
    for rel in range(4):
        m[:, rel * TC:(rel + 1) * TC] = (rel * 128 + k <= q).astype(np.float32)
    return m


def _host_inputs(x, wq, wk, wv, wo, route_w, A, Bw):
    C2, S2 = _rope_tables()
    maskT = _masks()
    Pmat = np.zeros((8, 128), np.float32)
    for e in range(8):
        Pmat[e, e * 16:(e + 1) * 16] = 1.0
    onesc = np.ones((128, 1), np.float32)
    onesr = np.ones((1, 128), np.float32)
    onesr8 = np.ones((1, 8), np.float32)
    ident = np.eye(128, dtype=np.float32)

    order = []
    for h in range(NH):
        order.append(wo[h * HD:(h + 1) * HD])
        order.append(wo[(NH + h) * HD:(NH + h + 1) * HD])
    wo_perm = np.concatenate(order, axis=0)
    wo_t = np.ascontiguousarray(
        wo_perm.reshape(32, 128, 32, 128).transpose(2, 1, 0, 3)
        .reshape(32, 128, 32 * 128))

    route_t = np.ascontiguousarray(
        route_w.T.reshape(32, 128, 8).transpose(1, 0, 2).reshape(128, 256))
    A_t = np.ascontiguousarray(
        A.reshape(128, D).T.reshape(32, 128, 128).transpose(1, 0, 2)
        .reshape(128, D))
    Bw_t = np.ascontiguousarray(Bw.transpose(0, 2, 1).reshape(128, D))

    maps = []
    for c in range(8):
        b, half = c // 2, c % 2
        cols = slice(half * 2048, half * 2048 + 2048)
        xT = np.ascontiguousarray(x[b].T)
        xc = np.ascontiguousarray(
            xT.reshape(32, 128, 8, TC).transpose(2, 1, 0, 3)
            .reshape(8, 128, 32 * TC))
        xTh = xT[:, half * SH:(half + 1) * SH]
        xrh = np.ascontiguousarray(
            xTh.reshape(32, 128, NU, TCA).transpose(2, 0, 1, 3))

        def wtile(w):
            return np.ascontiguousarray(
                w[:, cols].reshape(32, 128, 16, 128).transpose(2, 1, 0, 3)
                .reshape(16, 128, 32 * 128))
        maps.append({
            "xc": xc,
            "xrh": xrh,
            "wq": wtile(wq),
            "wk": wtile(wk),
            "wv": wtile(wv),
            "wo": wo_t,
            "route": route_t, "Am": A_t, "Bwm": Bw_t,
            "C2": C2, "S2": S2, "maskT": maskT, "Pmat": Pmat,
            "onesc": onesc, "onesr": onesr, "onesr8": onesr8, "ident": ident,
            "tok0": np.array([[half * SH]], np.int32),
        })
    return maps



def build_program():
    nc = bass.Bass("TRN2", target_bir_lowering=False,
                   detect_race_conditions=False)

    xc = nc.dram_tensor("xc", [NC1, 128, NDC * TC], F32R,
                        kind="ExternalInput").ap()
    xrh = nc.dram_tensor("xrh", [NU, NDC, 128, TCA], F32R,
                         kind="ExternalInput").ap()
    wq = nc.dram_tensor("wq", [NOT, 128, NDC * 128], F32R, kind="ExternalInput").ap()
    wk = nc.dram_tensor("wk", [NOT, 128, NDC * 128], F32R, kind="ExternalInput").ap()
    wv = nc.dram_tensor("wv", [NOT, 128, NDC * 128], F32R, kind="ExternalInput").ap()
    wo = nc.dram_tensor("wo", [32, 128, NDC * 128], F32R, kind="ExternalInput").ap()
    route = nc.dram_tensor("route", [128, NDC * 8], F32R, kind="ExternalInput").ap()
    Am = nc.dram_tensor("Am", [128, NDC * 128], F32R, kind="ExternalInput").ap()
    Bwm = nc.dram_tensor("Bwm", [128, NDC * 128], F32R, kind="ExternalInput").ap()
    C2h = nc.dram_tensor("C2", [HD, S], F32, kind="ExternalInput").ap()
    S2h = nc.dram_tensor("S2", [HD, S], F32, kind="ExternalInput").ap()
    maskTh = nc.dram_tensor("maskT", [HD, 4 * TC], F32, kind="ExternalInput").ap()
    Pmath = nc.dram_tensor("Pmat", [8, 128], F32R, kind="ExternalInput").ap()
    onesch = nc.dram_tensor("onesc", [128, 1], F32R, kind="ExternalInput").ap()
    onesrh = nc.dram_tensor("onesr", [1, 128], F32R, kind="ExternalInput").ap()
    onesr8h = nc.dram_tensor("onesr8", [1, 8], F32R, kind="ExternalInput").ap()
    identh = nc.dram_tensor("ident", [128, 128], F32R, kind="ExternalInput").ap()
    OUT = nc.dram_tensor("out", [D, SH], F32, kind="ExternalOutput").ap()

    kind_s = "ExternalOutput" if DBG else "Internal"
    QT = nc.dram_tensor("QT", [NH, 128, S], F32R, kind=kind_s).ap()
    KT = nc.dram_tensor("KT", [NH, 128, S], F32R, kind=kind_s).ap()
    VT = nc.dram_tensor("VT", [NH, 128, S], F32R, kind=kind_s).ap()
    OUT1 = nc.dram_tensor("OUT1", [D, SH], F32R, kind=kind_s).ap()
    a2a_in = nc.dram_tensor("a2a_in", [NH, 128, S], F32R, kind="Internal")
    a2a_out = nc.dram_tensor("a2a_out", [NH, 256, S], F32R, kind="Internal")
    tok0h = nc.dram_tensor("tok0", [1, 1], mybir.dt.int32,
                           kind="ExternalInput").ap()

    with ExitStack() as st:
        def sem(name):
            return st.enter_context(nc.semaphore(name))

        s_x = sem("s_x"); s_w = sem("s_w"); s_w2 = sem("s_w2")
        s_v2 = sem("s_v2")
        s_mm2 = sem("s_mm2"); s_vst = sem("s_vst"); s_sst = sem("s_sst")
        s_mmb = sem("s_mmb"); s_vxn = sem("s_vxn"); s_mmp = sem("s_mmp")
        s_ve = sem("s_ve"); s_scw = sem("s_scw")
        s_hl = sem("s_hl"); s_mk = sem("s_mk"); s_mmt = sem("s_mmt")
        s_vt = sem("s_vt"); s_mms = sem("s_mms"); s_act = sem("s_act")
        s_vm = sem("s_vm"); s_pv = sem("s_pv"); s_qc = sem("s_qc")
        s_vr = sem("s_vr"); s_mmo = sem("s_mmo"); s_vat = sem("s_vat")
        s_aw = sem("s_aw"); s_cc = sem("s_cc")
        s_al = sem("s_al"); s_xr = sem("s_xr"); s_w3 = sem("s_w3")
        s_mmP = sem("s_mmP"); s_vo1 = sem("s_vo1"); s_o1w = sem("s_o1w")
        s_ol = sem("s_ol"); s_ab = sem("s_ab"); s_v23 = sem("s_v23")
        s_m23 = sem("s_m23"); s_vst3 = sem("s_vst3"); s_sst3 = sem("s_sst3")
        s_mb3 = sem("s_mb3"); s_vx3 = sem("s_vx3"); s_mlg = sem("s_mlg")
        s_vlg = sem("s_vlg"); s_mrw = sem("s_mrw"); s_vrw = sem("s_vrw")
        s_mh = sem("s_mh"); s_vwh = sem("s_vwh"); s_mf3 = sem("s_mf3")
        s_vf3 = sem("s_vf3"); s_out = sem("s_out")

        block = st.enter_context(nc.Block())

        onesc_sb = st.enter_context(nc.sbuf_tensor("onesc_sb", [128, 1], F32R))
        onesr_sb = st.enter_context(nc.sbuf_tensor("onesr_sb", [1, 128], F32R))
        onesr8_sb = st.enter_context(nc.sbuf_tensor("onesr8_sb", [1, 8], F32R))
        ident_sb = st.enter_context(nc.sbuf_tensor("ident_sb", [128, 128], F32R))
        Pmat_sb = st.enter_context(nc.sbuf_tensor("Pmat_sb", [8, 128], F32R))
        route_sb = st.enter_context(nc.sbuf_tensor("route_sb", [128, NDC * 8], F32R))
        mu_sb = st.enter_context(nc.sbuf_tensor("mu_sb", [1, TC], F32R))
        m2_sb = st.enter_context(nc.sbuf_tensor("m2_sb", [1, TC], F32))
        mu2_sb = st.enter_context(nc.sbuf_tensor("mu2_sb", [1, TC], F32))
        var_sb = st.enter_context(nc.sbuf_tensor("var_sb", [1, TC], F32))
        rinv_sb = st.enter_context(nc.sbuf_tensor("rinv_sb", [1, TC], F32))
        rstd_sb = st.enter_context(nc.sbuf_tensor("rstd_sb", [1, TC], F32R))
        mub_sb = st.enter_context(nc.sbuf_tensor("mub_sb", [128, TC], F32))
        cm05_sb = st.enter_context(nc.sbuf_tensor("cm05_sb", [128, 1], F32))
        rstdb_sb = st.enter_context(nc.sbuf_tensor("rstdb_sb", [128, TC], F32))

        @block.gpsimd
        def _(g):
            g.dma_start(out=onesc_sb[:], in_=onesch).then_inc(s_scw, 16)
            g.dma_start(out=onesr_sb[:], in_=onesrh).then_inc(s_scw, 16)
            g.dma_start(out=onesr8_sb[:], in_=onesr8h).then_inc(s_scw, 16)
            g.dma_start(out=ident_sb[:], in_=identh).then_inc(s_scw, 16)
            g.dma_start(out=Pmat_sb[:], in_=Pmath).then_inc(s_scw, 16)
            g.dma_start(out=route_sb[:], in_=route).then_inc(s_scw, 16)
        NCONST = 6

        p1 = ExitStack()
        C2_sb = p1.enter_context(nc.sbuf_tensor("C2_sb", [HD, S], F32))
        S2_sb = p1.enter_context(nc.sbuf_tensor("S2_sb", [HD, S], F32))
        x_sb = p1.enter_context(nc.sbuf_tensor("x_sb", [128, NDC * TC], F32R))
        x2f_sb = p1.enter_context(nc.sbuf_tensor("x2f_sb", [128, 4 * TC], F32R))
        w_sb = p1.enter_context(nc.sbuf_tensor("w_sb", [128, 3 * NDC * 128], F32R))
        ebuf_sb = p1.enter_context(nc.sbuf_tensor("ebuf_sb", [128, 4 * TC], F32R))
        rt1_sb = p1.enter_context(nc.sbuf_tensor("rt1_sb", [128, TC], F32))
        rt2_sb = p1.enter_context(nc.sbuf_tensor("rt2_sb", [128, TC], F32))
        tmp_sb = p1.enter_context(nc.sbuf_tensor("tmp_sb", [128, TC], F32))
        ps_stat = p1.enter_context(nc.psum_tensor("ps_stat", [1, 1024], F32))
        ps_bc = p1.enter_context(nc.psum_tensor("ps_bc", [128, 1024], F32))
        ps_p = p1.enter_context(nc.psum_tensor("ps_p", [128, 2048], F32))

        NBLK = 3 * NOT
        W_ALL = [wq, wk, wv]
        SWAP = [i ^ 1 for i in range(32)]

        @block.sync
        def _(sync):
            for tc in range(NC1):
                for widx in range(3):
                    for ot in range(NOT):
                        g = tc * NBLK + widx * NOT + ot
                        if g % 2 != 1:
                            continue
                        if g >= 3:
                            sync.wait_ge(s_mmp, g - 2)
                        par = g % 3
                        sync.dma_start(
                            out=w_sb[:, par * NDC * 128:(par + 1) * NDC * 128],
                            in_=W_ALL[widx][ot],
                        ).then_inc(s_w2, 16)

        @block.scalar
        def _(sc):
            sc.dma_start(out=C2_sb[:], in_=C2h).then_inc(s_w, 16)
            sc.dma_start(out=S2_sb[:], in_=S2h).then_inc(s_w, 16)
            for tc in range(NC1):
                for widx in range(3):
                    for ot in range(NOT):
                        g = tc * NBLK + widx * NOT + ot
                        if g % 2 == 0:
                            if g >= 3:
                                sc.wait_ge(s_mmp, g - 2)
                            par = g % 3
                            sc.dma_start(
                                out=w_sb[:, par * NDC * 128:
                                         (par + 1) * NDC * 128],
                                in_=W_ALL[widx][ot],
                            ).then_inc(s_w, 16)
                        if widx == 0 and ot == 0:
                            sc.wait_ge(s_vst, tc + 1)
                            sc.activation(rinv_sb[:], var_sb[:], ACT.Ln)
                            sc.activation(rstd_sb[:], rinv_sb[:], ACT.Exp,
                                          scale=cm05_sb[0:1, :]
                                          ).then_inc(s_sst, 1)

        @block.vector
        def _(v):
            v.memset(cm05_sb[:], -0.5)
            for tc in range(NC1):
                v.wait_ge(s_x, 16 * (tc + 1))
                for g4 in range(8):
                    gi = tc * 8 + g4
                    if gi > 0:
                        v.wait_ge(s_mm2, gi)
                    for j in range(4):
                        dc = g4 * 4 + j
                        ins = v.tensor_mul(
                            x2f_sb[:, (dc % 4) * TC:(dc % 4 + 1) * TC],
                            x_sb[:, dc * TC:(dc + 1) * TC].bitcast(F32),
                            x_sb[:, dc * TC:(dc + 1) * TC].bitcast(F32))
                    ins.then_inc(s_v2, 1)
                v.wait_ge(s_mm2, tc * 8 + 8)
                v.tensor_scalar(mu_sb[:], ps_stat[0:1, 0:TC], 1.0 / D, None,
                                ALU.mult)
                v.tensor_scalar(m2_sb[:], ps_stat[0:1, TC:2 * TC], 1.0 / D,
                                None, ALU.mult)
                v.tensor_mul(mu2_sb[:], mu_sb[:].bitcast(F32),
                             mu_sb[:].bitcast(F32))
                ins = v.scalar_tensor_tensor(var_sb[:], m2_sb[:], EPS,
                                             mu2_sb[:], ALU.add, ALU.subtract)
                ins.then_inc(s_vst, 1)
                v.wait_ge(s_mmb, tc + 1)
                v.tensor_copy(mub_sb[:], ps_bc[:, 0:TC])
                v.tensor_copy(rstdb_sb[:], ps_bc[:, TC:2 * TC])
                for dc in range(NDC):
                    sl = x_sb[:, dc * TC:(dc + 1) * TC]
                    v.tensor_sub(tmp_sb[:], sl.bitcast(F32), mub_sb[:])
                    ins = v.tensor_mul(sl, tmp_sb[:], rstdb_sb[:])
                ins.then_inc(s_vxn, 1)
                for widx in range(3):
                    for ot in range(NOT):
                        g = tc * NBLK + widx * NOT + ot
                        v.wait_ge(s_mmp, g + 1)
                        if g >= 4:
                            v.wait_ge(s_scw, 16 * (g - 3) + 16 * NCONST)
                        par = g % 4
                        eb = ebuf_sb[:, (g % 4) * TC:(g % 4 + 1) * TC]
                        psrc = ps_p[:, par * TC:(par + 1) * TC]
                        if widx == 2:
                            ins = v.tensor_copy(eb, psrc)
                        else:
                            cs = C2_sb[:, tc * TC:(tc + 1) * TC]
                            ss = S2_sb[:, tc * TC:(tc + 1) * TC]
                            v.tensor_mul(rt1_sb[:], psrc, cs)
                            v.tensor_mul(rt2_sb[:], psrc, ss)
                            v.stream_shuffle(tmp_sb[:], rt2_sb[:], SWAP)
                            ins = v.tensor_add(eb, rt1_sb[:], tmp_sb[:])
                        ins.then_inc(s_ve, 1)

        @block.tensor
        def _(t):
            t.wait_ge(s_scw, 16 * NCONST)
            for tc in range(NC1):
                t.wait_ge(s_x, 16 * (tc + 1))
                for dc in range(NDC):
                    t.matmul(ps_stat[0:1, 0:TC], onesc_sb[:],
                             x_sb[:, dc * TC:(dc + 1) * TC],
                             start=(dc == 0), stop=(dc == NDC - 1))
                for g4 in range(8):
                    t.wait_ge(s_v2, tc * 8 + g4 + 1)
                    for j in range(4):
                        dc = g4 * 4 + j
                        ins = t.matmul(ps_stat[0:1, TC:2 * TC], onesc_sb[:],
                                       x2f_sb[:, (dc % 4) * TC:(dc % 4 + 1) * TC],
                                       start=(dc == 0), stop=(dc == NDC - 1))
                    ins.then_inc(s_mm2, 1)
                t.wait_ge(s_vst, tc + 1)
                t.wait_ge(s_sst, tc + 1)
                t.matmul(ps_bc[:, 0:TC], onesr_sb[:], mu_sb[:],
                         start=True, stop=True)
                t.matmul(ps_bc[:, TC:2 * TC], onesr_sb[:], rstd_sb[:],
                         start=True, stop=True).then_inc(s_mmb, 1)
                t.wait_ge(s_vxn, tc + 1)
                for widx in range(3):
                    for ot in range(NOT):
                        g = tc * NBLK + widx * NOT + ot
                        if g % 2 == 0:
                            t.wait_ge(s_w, 16 * (g // 2 + 1) + 32)
                        else:
                            t.wait_ge(s_w2, 16 * (g // 2 + 1))
                        if g >= 4:
                            t.wait_ge(s_ve, g - 3)
                        par = g % 4
                        for dc in range(NDC):
                            ins = t.matmul(
                                ps_p[:, par * TC:(par + 1) * TC],
                                w_sb[:, ((g % 3) * NDC + dc) * 128:
                                     ((g % 3) * NDC + dc + 1) * 128],
                                x_sb[:, dc * TC:(dc + 1) * TC],
                                start=(dc == 0), stop=(dc == NDC - 1))
                        ins.then_inc(s_mmp, 1)

        @block.gpsimd
        def _(g):
            SCR = [QT, KT, VT]
            for tc in range(NC1):
                if tc > 0:
                    g.wait_ge(s_mmp, NBLK * tc)
                g.dma_start(out=x_sb[:], in_=xc[tc]).then_inc(s_x, 16)
                for widx in range(3):
                    for ot in range(NOT):
                        gg = tc * NBLK + widx * NOT + ot
                        g.wait_ge(s_ve, gg + 1)
                        g.dma_start(
                            out=SCR[widx][ot, :, tc * TC:(tc + 1) * TC],
                            in_=ebuf_sb[:, (gg % 4) * TC:(gg % 4 + 1) * TC],
                        ).then_inc(s_scw, 16)

        P1_SCW = 16 * (NCONST + NC1 * NBLK)
        p1.close()

        p2 = ExitStack()
        qh_sb = p2.enter_context(nc.sbuf_tensor("qh_sb", [128, 2 * S], F32R))
        kh_sb = p2.enter_context(nc.sbuf_tensor("kh_sb", [128, 2 * S], F32R))
        vh_sb = p2.enter_context(nc.sbuf_tensor("vh_sb", [128, 2 * S], F32R))
        vtm_sb = p2.enter_context(nc.sbuf_tensor("vtm_sb", [128, S], F32R))
        maskT_sb = p2.enter_context(nc.sbuf_tensor("maskT_sb", [HD, 4 * TC], F32))
        prb_sb = p2.enter_context(nc.sbuf_tensor("prb_sb", [128, 4 * TC], F32R))
        attb_sb = p2.enter_context(nc.sbuf_tensor("attb_sb", [128, 2 * TC], F32R))
        linv_sb = p2.enter_context(nc.sbuf_tensor("linv_sb", [1, TC], F32R))
        lb_sb = p2.enter_context(nc.sbuf_tensor("lb_sb", [128, TC], F32))
        ps_s = p2.enter_context(nc.psum_tensor("ps_s", [128, 4 * TC], F32))
        ps_att = p2.enter_context(nc.psum_tensor("ps_att", [128, 2 * TC], F32))
        ps_l = p2.enter_context(nc.psum_tensor("ps_l", [1, TC], F32))
        ps_b = p2.enter_context(nc.psum_tensor("ps_b", [128, TC], F32))
        ps_t = ps_s.bitcast(F32R) if False else None
        cbias_sb = p2.enter_context(nc.sbuf_tensor("cbias_sb", [128, 1], F32))
        cscl_sb = p2.enter_context(nc.sbuf_tensor("cscl_sb", [128, 1], F32))
        csm1_sb = p2.enter_context(nc.sbuf_tensor("csm1_sb", [128, 1], F32))
        lnt_sb = p2.enter_context(nc.sbuf_tensor("lnt_sb", [1, TC], F32))

        KTS = [4 * qc + 4 for qc in range(8)]
        KT_HEAD = sum(KTS)
        PRS = [k // 2 for k in KTS]
        PR_HEAD = sum(PRS)

        def kt_base(h, qc):
            return h * KT_HEAD + sum(KTS[:qc])

        def pr_base(h, qc):
            return h * PR_HEAD + sum(PRS[:qc])

        @block.sync
        def _(sync):
            sync.wait_ge(s_scw, P1_SCW)
            for h in range(NH):
                par = h % 2
                if h >= 2:
                    sync.wait_ge(s_qc, 8 * (h - 1))
                sync.dma_start(out=qh_sb[:, par * S:(par + 1) * S],
                               in_=QT[h]).then_inc(s_hl, 16)
                sync.dma_start(out=kh_sb[:, par * S:(par + 1) * S],
                               in_=KT[h]).then_inc(s_hl, 16)
                sync.dma_start(out=vh_sb[:, par * S:(par + 1) * S],
                               in_=VT[h]).then_inc(s_hl, 16)

        @block.scalar
        def _(sc):
            sc.wait_ge(s_scw, P1_SCW)
            sc.dma_start(out=maskT_sb[:], in_=maskTh).then_inc(s_mk, 16)
            sc.wait_ge(s_mk, 17)
            for h in range(NH):
                for qc in range(8):
                    for j in range(PRS[qc]):
                        gp = pr_base(h, qc) + j
                        sc.wait_ge(s_mms, gp + 1)
                        if gp >= 2:
                            sc.wait_ge(s_pv, 2 * (gp - 1))
                        sl2 = slice((gp % 2) * 2 * TC, (gp % 2 + 1) * 2 * TC)
                        sc.activation(prb_sb[:, sl2], ps_s[:, sl2],
                                      ACT.Exp, bias=cbias_sb[:],
                                      scale=cscl_sb[:]).then_inc(s_act, 1)
                    nq = h * 8 + qc
                    sc.wait_ge(s_qc, nq + 1)
                    sc.activation(lnt_sb[:], ps_l[:], ACT.Ln)
                    sc.activation(linv_sb[:], lnt_sb[:], ACT.Exp,
                                  scale=csm1_sb[0:1, :]).then_inc(s_vr, 1)

        @block.vector
        def _(v):
            v.wait_ge(s_scw, P1_SCW)
            v.wait_ge(s_mk, 16)
            v.memset(cbias_sb[:], -C0)
            v.memset(csm1_sb[:], -1.0)
            v.memset(cscl_sb[:], SCL).then_inc(s_mk, 1)
            for h in range(NH):
                for j in range(NDC):
                    v.wait_ge(s_mmt, h * NDC + j + 1)
                    v.tensor_copy(vtm_sb[:, j * 128:(j + 1) * 128],
                                  ps_s[:, (j % 2) * 512:(j % 2) * 512 + 128])
                    v.engine_nop().then_inc(s_vt, 1)
                for qc in range(8):
                    nq = h * 8 + qc
                    for rel in range(4):
                        kt = 4 * qc + rel
                        n = kt_base(h, qc) + kt
                        v.wait_ge(s_act, pr_base(h, qc) + kt // 2 + 1)
                        gp = pr_base(h, qc) + kt // 2
                        off = ((gp % 2) * 2 + kt % 2) * TC
                        sl = prb_sb[:, off:off + TC]
                        v.tensor_mul(sl, sl.bitcast(F32),
                                     maskT_sb[:, rel * TC:(rel + 1) * TC]
                                     ).then_inc(s_vm, 1)
                    v.wait_ge(s_mmo, nq + 1)
                    v.tensor_copy(lb_sb[:], ps_b[:])
                    ab = attb_sb[:, (qc % 2) * TC:(qc % 2 + 1) * TC]
                    if nq >= 2:
                        v.wait_ge(s_aw, 16 * (nq - 1))
                    v.wait_ge(s_pv, kt_base(h, qc) + KTS[qc])
                    v.tensor_mul(ab, ps_att[:, (qc % 2) * TC:(qc % 2 + 1) * TC],
                                 lb_sb[:]).then_inc(s_vat, 1)

        def t_wait_probs(t, h, qc, kt, n):
            if kt >= 4 * qc:
                ndv = (h * 8 + qc) * 4 + (kt - 4 * qc) + 1
                t.wait_ge(s_vm, ndv)
            else:
                t.wait_ge(s_act, pr_base(h, qc) + kt // 2 + 1)

        def emit_lsum_pv(t, h, qc, kt, n, nq):
            gp = pr_base(h, qc) + kt // 2
            off = ((gp % 2) * 2 + kt % 2) * TC
            prb = prb_sb[:, off:off + TC]
            nkts = KTS[qc]
            if kt == 0:
                if nq >= 1:
                    t.wait_ge(s_vr, nq)
                if nq >= 2:
                    t.wait_ge(s_vat, nq - 1)
            ins = t.matmul(ps_l[:], onesc_sb[:], prb,
                           start=(kt == 0), stop=(kt == nkts - 1))
            if kt == nkts - 1:
                ins.then_inc(s_qc, 1)
            t.matmul(ps_att[:, (qc % 2) * TC:(qc % 2 + 1) * TC],
                     vtm_sb[:, kt * 128:(kt + 1) * 128], prb,
                     start=(kt == 0), stop=(kt == nkts - 1)).then_inc(s_pv, 1)

        @block.tensor
        def _(t):
            for h in range(NH):
                par = h % 2
                t.wait_ge(s_hl, 48 * (h + 1))
                if h >= 1:
                    t.wait_ge(s_act, h * PR_HEAD)
                for j in range(NDC):
                    if h * NDC + j >= 2:
                        t.wait_ge(s_vt, h * NDC + j - 1)
                    t.transpose(ps_s.bitcast(F32R)[:, (j % 2) * 512:
                                                   (j % 2) * 512 + 128],
                                vh_sb[:, par * S + j * 128:
                                      par * S + (j + 1) * 128],
                                ident_sb[:]).then_inc(s_mmt, 1)
                t.wait_ge(s_vt, (h + 1) * NDC)
                for qc in range(8):
                    nq = h * 8 + qc
                    nprs = PRS[qc]
                    for j in range(nprs):
                        gp = pr_base(h, qc) + j
                        if gp >= 2:
                            t.wait_ge(s_act, gp - 1)
                        for half2 in range(2):
                            kt = 2 * j + half2
                            ins = t.matmul(
                                ps_s[:, ((gp % 2) * 2 + half2) * TC:
                                     ((gp % 2) * 2 + half2 + 1) * TC],
                                kh_sb[:, par * S + kt * 128:
                                      par * S + (kt + 1) * 128],
                                qh_sb[:, par * S + qc * TC:
                                      par * S + (qc + 1) * TC],
                                start=True, stop=True)
                        ins.then_inc(s_mms, 1)
                        if j >= 1:
                            for half2 in range(2):
                                kt = 2 * (j - 1) + half2
                                n = kt_base(h, qc) + kt
                                t_wait_probs(t, h, qc, kt, n)
                                emit_lsum_pv(t, h, qc, kt, n, nq)
                        if j == 1 and qc >= 1:
                            t.wait_ge(s_vr, nq)
                            t.wait_ge(s_vat, nq - 1 if nq >= 2 else 0)
                            t.matmul(ps_b[:], onesr_sb[:], linv_sb[:],
                                     start=True, stop=True).then_inc(s_mmo, 1)
                    for half2 in range(2):
                        kt = 2 * (nprs - 1) + half2
                        n = kt_base(h, qc) + kt
                        t_wait_probs(t, h, qc, kt, n)
                        emit_lsum_pv(t, h, qc, kt, n, nq)
                t.wait_ge(s_vr, 8 * (h + 1))
                t.wait_ge(s_vat, 8 * (h + 1) - 1)
                t.matmul(ps_b[:], onesr_sb[:], linv_sb[:],
                         start=True, stop=True).then_inc(s_mmo, 1)

        @block.gpsimd
        def _(g):
            for h in range(NH):
                for qc in range(8):
                    nq = h * 8 + qc
                    g.wait_ge(s_vat, nq + 1)
                    g.dma_start(
                        out=a2a_in.ap()[h, :, qc * TC:(qc + 1) * TC],
                        in_=attb_sb[:, (qc % 2) * TC:(qc % 2 + 1) * TC],
                    ).then_inc(s_aw, 16)
                g.wait_ge(s_aw, 16 * 8 * (h + 1))
                g.collective_compute(
                    "AllGather", ALU.bypass,
                    replica_groups=GROUPS,
                    ins=[a2a_in.ap()[h].opt()],
                    outs=[a2a_out.ap()[h].opt()],
                ).then_inc(s_cc, 1)

        p2.close()

        p3 = ExitStack()
        att_sb = p3.enter_context(nc.sbuf_tensor("att_sb", [128, NDC * TCA], F32R))
        wod_sb = p3.enter_context(nc.sbuf_tensor("wod_sb", [128, 2 * NDC * 128], F32R))
        xr_sb = p3.enter_context(nc.sbuf_tensor("xr_sb", [128, 2 * TCA], F32R))
        eb3_sb = p3.enter_context(nc.sbuf_tensor("eb3_sb", [128, 2 * TCA], F32R))
        ps_P = p3.enter_context(nc.psum_tensor("ps_P", [128, 2048], F32))

        tok0_sb = p3.enter_context(nc.sbuf_tensor("tok0_sb", [1, 1],
                                                   mybir.dt.int32))

        @block.sync
        def _(sync):
            sync.wait_ge(s_cc, NH)
            for u in range(NU):
                for dcol in range(NDC):
                    g3 = u * NDC + dcol
                    if g3 >= 2:
                        sync.wait_ge(s_vo1, g3 - 1)
                    sync.dma_start(
                        out=xr_sb[:, (dcol % 2) * TCA:(dcol % 2 + 1) * TCA],
                        in_=xrh[u, dcol],
                    ).then_inc(s_xr, 16)

        @block.scalar
        def _(sc):
            sc.wait_ge(s_cc, NH)
            for u in range(NU):
                for dcol in range(NDC):
                    g3 = u * NDC + dcol
                    if g3 >= 2:
                        sc.wait_ge(s_mmP, g3 - 1)
                    par = g3 % 2
                    sc.dma_start(
                        out=wod_sb[:, par * NDC * 128:(par + 1) * NDC * 128],
                        in_=wo[dcol],
                    ).then_inc(s_w3, 16)

        @block.tensor
        def _(t):
            for u in range(NU):
                for dcol in range(NDC):
                    g3 = u * NDC + dcol
                    t.wait_ge(s_w3, 16 * (g3 + 1))
                    if g3 >= 2:
                        t.wait_ge(s_vo1, g3 - 1)
                    par = g3 % 2
                    for hf in range(2):
                        for rc in range(NDC):
                            if dcol == 0 and hf == 0:
                                t.wait_ge(s_al, 16 * (1 + NH * u + rc // 2 + 1))
                            ins = t.matmul(
                                ps_P[:, (par * 2 + hf) * TC:
                                     (par * 2 + hf + 1) * TC],
                                wod_sb[:, (par * NDC + rc) * 128:
                                       (par * NDC + rc + 1) * 128],
                                att_sb[:, rc * TCA + hf * TC:
                                       rc * TCA + (hf + 1) * TC],
                                start=(rc == 0), stop=(rc == NDC - 1))
                    ins.then_inc(s_mmP, 1)

        @block.vector
        def _(v):
            v.wait_ge(s_cc, NH)
            for u in range(NU):
                for dcol in range(NDC):
                    g3 = u * NDC + dcol
                    v.wait_ge(s_mmP, g3 + 1)
                    v.wait_ge(s_xr, 16 * (g3 + 1))
                    if g3 >= 2:
                        v.wait_ge(s_o1w, 32 * (g3 - 1))
                    par = g3 % 2
                    v.tensor_add(eb3_sb[:, (g3 % 2) * TCA:(g3 % 2 + 1) * TCA],
                                 ps_P[:, par * 2 * TC:(par + 1) * 2 * TC],
                                 xr_sb[:, (dcol % 2) * TCA:(dcol % 2 + 1) * TCA]
                                 .bitcast(F32)).then_inc(s_vo1, 1)

        @block.gpsimd
        def _(g):
            g.wait_ge(s_cc, NH)
            g.dma_start(out=tok0_sb[:], in_=tok0h).then_inc(s_al, 16)
            g.wait_ge(s_al, 16)
            with g.register("gr3") as gr3:
                g.reg_load(gr3, tok0_sb[0:1, 0:1])
                for u in range(NU):
                    if u > 0:
                        g.wait_ge(s_mmP, NDC * u)
                    def att_piece(w0, h):
                        return a2a_out.ap()[h][:, w0 + u * TCA:
                                               w0 + (u + 1) * TCA].rearrange(
                            "(b p) t -> b p t", p=128).transpose([1, 0, 2])
                    def emit_pieces(w0):
                        for h in range(NH):
                            g.dma_start(
                                out=att_sb[:, 2 * h * TCA:(2 * h + 2) * TCA],
                                in_=att_piece(w0, h)).then_inc(s_al, 16)
                    with g.If_eq(gr3, 0):
                        emit_pieces(0)
                    with g.Else():
                        emit_pieces(SH)
                    for dcol in range(NDC):
                        g3 = u * NDC + dcol
                        g.wait_ge(s_vo1, g3 + 1)
                        g.dma_start(
                            out=OUT1[dcol * 128:(dcol + 1) * 128,
                                     u * TCA:(u + 1) * TCA],
                            in_=eb3_sb[:, (g3 % 2) * TCA:(g3 % 2 + 1) * TCA],
                        ).then_inc(s_o1w, 32)

        P3A_O1W = 32 * NU * NDC
        p3.close()

        p4 = ExitStack()
        o1_sb = p4.enter_context(nc.sbuf_tensor("o1_sb", [128, 2 * NDC * T3], F32R))
        xn2_sb = p4.enter_context(nc.sbuf_tensor("xn2_sb", [128, NDC * T3], F32R))
        ab_sb = p4.enter_context(nc.sbuf_tensor("ab_sb", [128, 2 * NDC * 128], F32R))
        x23_sb = p4.enter_context(nc.sbuf_tensor("x23_sb", [128, 4 * T3], F32R))
        lgp_sb = p4.enter_context(nc.sbuf_tensor("lgp_sb", [32, T3], F32))
        lgs_sb = p4.enter_context(nc.sbuf_tensor("lgs_sb", [32, T3], F32))
        mx_sb = p4.enter_context(nc.sbuf_tensor("mx_sb", [32, T3], F32))
        m1r_sb = p4.enter_context(nc.sbuf_tensor("m1r_sb", [1, T3], F32R))
        mb8_sb = p4.enter_context(nc.sbuf_tensor("mb8_sb", [8, T3], F32))
        rwm_sb = p4.enter_context(nc.sbuf_tensor("rwm_sb", [8, T3], F32))
        rw_sb = p4.enter_context(nc.sbuf_tensor("rw_sb", [8, T3], F32R))
        rwb_sb = p4.enter_context(nc.sbuf_tensor("rwb_sb", [128, T3], F32))
        wh_sb = p4.enter_context(nc.sbuf_tensor("wh_sb", [128, T3], F32R))
        tmp3_sb = p4.enter_context(nc.sbuf_tensor("tmp3_sb", [128, T3], F32))
        fin_sb = p4.enter_context(nc.sbuf_tensor("fin_sb", [128, NDC * T3], F32))
        ps_f = p4.enter_context(nc.psum_tensor("ps_f", [128, 1024], F32))
        ps_st3 = p4.enter_context(nc.psum_tensor("ps_st3", [1, 1024], F32))
        ps_bc3 = p4.enter_context(nc.psum_tensor("ps_bc3", [128, 1024], F32))
        ps_ms = p4.enter_context(nc.psum_tensor("ps_ms", [128, 512], F32))

        NG3 = 8

        @block.gpsimd
        def _(g):
            g.wait_ge(s_o1w, P3A_O1W)
            g.dma_start(out=ab_sb[:, 0:NDC * 128],
                        in_=Am).then_inc(s_ab, 16)
            g.dma_start(out=ab_sb[:, NDC * 128:2 * NDC * 128],
                        in_=Bwm).then_inc(s_ab, 16)
            for u in range(NU3):
                g.wait_ge(s_vf3, NDC * (u + 1))
                g.dma_start(
                    out=OUT[:, u * T3:(u + 1) * T3].rearrange(
                        "(c p) t -> c p t", p=128).transpose([1, 0, 2]),
                    in_=fin_sb[:],
                ).then_inc(s_out, 16)
            g.wait_ge(s_out, 16 * NU3)

        @block.sync
        def _(sync):
            sync.wait_ge(s_o1w, P3A_O1W)
            for u in range(NU3):
                if u > 1:
                    sync.wait_ge(s_vf3, NDC * (u - 1))
                sync.dma_start(
                    out=o1_sb[:, (u % 2) * NDC * T3:(u % 2 + 1) * NDC * T3],
                    in_=OUT1[:, u * T3:(u + 1) * T3].rearrange(
                        "(c p) t -> c p t", p=128).transpose([1, 0, 2]),
                ).then_inc(s_ol, 16)

        @block.scalar
        def _(sc):
            for u in range(NU3):
                sc.wait_ge(s_vst3, u + 1)
                sc.activation(rinv_sb[0:1, 0:T3], var_sb[0:1, 0:T3], ACT.Ln)
                sc.activation(rstd_sb[0:1, 0:T3], rinv_sb[0:1, 0:T3],
                              ACT.Exp, scale=cm05_sb[0:1, :]
                              ).then_inc(s_sst3, 1)

        @block.vector
        def _(v):
            for u in range(NU3):
                v.wait_ge(s_ol, 16 * (u + 1))
                for g4 in range(NG3):
                    gi3 = u * NG3 + g4
                    if gi3 > 0:
                        v.wait_ge(s_m23, gi3)
                    for j in range(4):
                        dc = g4 * 4 + j
                        o1t = o1_sb[:, ((u % 2) * NDC + dc) * T3:
                                    ((u % 2) * NDC + dc + 1) * T3]
                        ins = v.tensor_mul(
                            x23_sb[:, (dc % 4) * T3:(dc % 4 + 1) * T3],
                            o1t.bitcast(F32), o1t.bitcast(F32))
                    ins.then_inc(s_v23, 1)
                v.wait_ge(s_m23, u * NG3 + NG3)
                v.tensor_scalar(mu_sb[0:1, 0:T3], ps_st3[0:1, 0:T3],
                                1.0 / D, None, ALU.mult)
                v.tensor_scalar(m2_sb[0:1, 0:T3], ps_st3[0:1, TC:TC + T3],
                                1.0 / D, None, ALU.mult)
                v.tensor_mul(mu2_sb[0:1, 0:T3], mu_sb[0:1, 0:T3].bitcast(F32),
                             mu_sb[0:1, 0:T3].bitcast(F32))
                ins = v.scalar_tensor_tensor(var_sb[0:1, 0:T3],
                                             m2_sb[0:1, 0:T3],
                                             EPS, mu2_sb[0:1, 0:T3],
                                             ALU.add, ALU.subtract)
                ins.then_inc(s_vst3, 1)
                v.wait_ge(s_mb3, u + 1)
                v.tensor_copy(mub_sb[:, 0:T3], ps_bc3[:, 0:T3])
                v.tensor_copy(rstdb_sb[:, 0:T3], ps_bc3[:, TC:TC + T3])
                for dc in range(NDC):
                    v.tensor_sub(tmp3_sb[:],
                                 o1_sb[:, ((u % 2) * NDC + dc) * T3:
                                       ((u % 2) * NDC + dc + 1) * T3]
                                 .bitcast(F32),
                                 mub_sb[:, 0:T3])
                    ins = v.tensor_mul(xn2_sb[:, dc * T3:(dc + 1) * T3],
                                       tmp3_sb[:], rstdb_sb[:, 0:T3])
                ins.then_inc(s_vx3, 1)
                v.wait_ge(s_mlg, u + 1)
                v.memset(lgp_sb[:], -3.0e38)
                v.tensor_copy(lgp_sb[0:8, :], ps_ms[0:8, 0:T3])
                v.stream_shuffle(lgs_sb[:], lgp_sb[:],
                                 [min(i + 4, 31) for i in range(32)])
                v.tensor_max(mx_sb[:], lgp_sb[:], lgs_sb[:])
                v.stream_shuffle(lgs_sb[:], mx_sb[:],
                                 [min(i + 2, 31) for i in range(32)])
                v.tensor_max(mx_sb[:], mx_sb[:], lgs_sb[:])
                v.stream_shuffle(lgs_sb[:], mx_sb[:],
                                 [min(i + 1, 31) for i in range(32)])
                v.tensor_max(mx_sb[:], mx_sb[:], lgs_sb[:])
                v.stream_shuffle(lgs_sb[:], mx_sb[:], [0] * 32)
                v.tensor_tensor(rwm_sb[:], lgp_sb[0:8, :], lgs_sb[0:8, :],
                                ALU.is_equal)
                v.tensor_mul(rw_sb[:], rwm_sb[:], lgs_sb[0:8, :]
                             ).then_inc(s_vlg, 1)
                v.wait_ge(s_mrw, u + 1)
                v.tensor_copy(rwb_sb[:], ps_ms[:, 0:T3]).then_inc(s_vrw, 1)
                v.wait_ge(s_mh, u + 1)
                v.tensor_mul(wh_sb[:], ps_bc3[:, 0:T3], rwb_sb[:]
                             ).then_inc(s_vwh, 1)
                if u > 0:
                    v.wait_ge(s_out, 16 * u)
                for ot in range(NDC):
                    nf = u * NDC + ot
                    v.wait_ge(s_mf3, nf + 1)
                    par = ot % 2
                    v.tensor_add(fin_sb[:, ot * T3:(ot + 1) * T3],
                                 ps_f[:, par * TC:par * TC + T3],
                                 o1_sb[:, ((u % 2) * NDC + ot) * T3:
                                       ((u % 2) * NDC + ot + 1) * T3]
                                 .bitcast(F32)).then_inc(s_vf3, 1)

        @block.tensor
        def _(t):
            t.wait_ge(s_ab, 32)
            for u in range(NU3):
                t.wait_ge(s_ol, 16 * (u + 1))
                for dc in range(NDC):
                    t.matmul(ps_st3[0:1, 0:T3], onesc_sb[:],
                             o1_sb[:, ((u % 2) * NDC + dc) * T3:
                                   ((u % 2) * NDC + dc + 1) * T3],
                             start=(dc == 0), stop=(dc == NDC - 1))
                for g4 in range(NG3):
                    t.wait_ge(s_v23, u * NG3 + g4 + 1)
                    for j in range(4):
                        dc = g4 * 4 + j
                        ins = t.matmul(ps_st3[0:1, TC:TC + T3], onesc_sb[:],
                                       x23_sb[:, (dc % 4) * T3:(dc % 4 + 1) * T3],
                                       start=(dc == 0), stop=(dc == NDC - 1))
                    ins.then_inc(s_m23, 1)
                t.wait_ge(s_vst3, u + 1)
                t.wait_ge(s_sst3, u + 1)
                t.matmul(ps_bc3[:, 0:T3], onesr_sb[:], mu_sb[0:1, 0:T3],
                         start=True, stop=True)
                t.matmul(ps_bc3[:, TC:TC + T3], onesr_sb[:],
                         rstd_sb[0:1, 0:T3],
                         start=True, stop=True).then_inc(s_mb3, 1)
                t.wait_ge(s_vx3, u + 1)
                for dc in range(NDC):
                    ins = t.matmul(ps_ms[0:8, 0:T3],
                                   route_sb[:, dc * 8:(dc + 1) * 8],
                                   xn2_sb[:, dc * T3:(dc + 1) * T3],
                                   start=(dc == 0), stop=(dc == NDC - 1))
                ins.then_inc(s_mlg, 1)
                t.wait_ge(s_vlg, u + 1)
                t.matmul(ps_ms[:, 0:T3], Pmat_sb[:], rw_sb[:],
                         start=True, stop=True).then_inc(s_mrw, 1)
                t.wait_ge(s_vrw, u + 1)
                for dc in range(NDC):
                    ins = t.matmul(ps_bc3[:, 0:T3],
                                   ab_sb[:, dc * 128:(dc + 1) * 128],
                                   xn2_sb[:, dc * T3:(dc + 1) * T3],
                                   start=(dc == 0), stop=(dc == NDC - 1))
                ins.then_inc(s_mh, 1)
                t.wait_ge(s_vwh, u + 1)
                for ot in range(NDC):
                    nf = u * NDC + ot
                    if nf >= 2:
                        t.wait_ge(s_vf3, nf - 1)
                    par = ot % 2
                    t.matmul(ps_f[:, par * TC:par * TC + T3],
                             ab_sb[:, (NDC + ot) * 128:(NDC + ot + 1) * 128],
                             wh_sb[:], start=True, stop=True
                             ).then_inc(s_mf3, 1)

        p4.close()

    return nc



def kernel(L0_attn_x, L0_attn_wq, L0_attn_wk, L0_attn_wv, L0_attn_wo,
           L0_ffn_route_w, L0_ffn_A, L0_ffn_Bw):
    x = np.ascontiguousarray(np.asarray(L0_attn_x, np.float32))
    maps = _host_inputs(
        x,
        np.ascontiguousarray(np.asarray(L0_attn_wq, np.float32)),
        np.ascontiguousarray(np.asarray(L0_attn_wk, np.float32)),
        np.ascontiguousarray(np.asarray(L0_attn_wv, np.float32)),
        np.ascontiguousarray(np.asarray(L0_attn_wo, np.float32)),
        np.ascontiguousarray(np.asarray(L0_ffn_route_w, np.float32)),
        np.ascontiguousarray(np.asarray(L0_ffn_A, np.float32)),
        np.ascontiguousarray(np.asarray(L0_ffn_Bw, np.float32)))
    nc = build_program()
    res = run_bass_kernel_spmd(nc, maps, core_ids=list(range(8)))
    out = np.empty((B, S, D), np.float32)
    for b in range(B):
        lo = res.results[2 * b]["out"]
        hi = res.results[2 * b + 1]["out"]
        out[b] = np.concatenate([lo, hi], axis=1).T
    return out
